# version 02
# baseline (speedup 1.0000x reference)
import numpy as np
import ml_dtypes
from contextlib import ExitStack

import concourse.bass as bass
import concourse.mybir as mybir
from concourse.bass_utils import run_bass_kernel_spmd

F32 = mybir.dt.float32
BF16 = mybir.dt.bfloat16
U8 = mybir.dt.uint8
AF = mybir.ActivationFunctionType
ALU = mybir.AluOpType
AX = mybir.AxisListType
BF = ml_dtypes.bfloat16

NC = 8
B, S, D = 2, 4096, 4096
T = B * S
TC = T // NC
KV = 1024
HEADS, KVH, HD = 32, 8, 128
NREP = HEADS // KVH
PAIRS_PER_CORE = (B * HEADS) // NC
NQC = S // 128
EPS = 1e-5
ISR = 1.0 / np.sqrt(128.0)
NEG = -3.0e38
NUM_REMAIN = 410



class Chain:

    def __init__(self):
        self.steps = []

    def _add(self, eng, inc, fn):
        self.steps.append((eng, inc, fn))

    def dma(self, fn, q="sync"):
        self._add(q, 16, fn)

    def v(self, fn):
        self._add("vector", 1, fn)

    def sc(self, fn):
        self._add("scalar", 1, fn)

    def te(self, fn, inc=1):
        self._add("tensor", inc, fn)

    def g(self, fn):
        self._add("gpsimd", 1, fn)

    def emit(self, nc, block, sem):
        pres = []
        tot = 0
        for (_, inc, _) in self.steps:
            pres.append(tot)
            tot += inc
        total = tot
        steps = self.steps

        def emit_for(eng_name, e):
            for (name, inc, fn), pre in zip(steps, pres):
                if name != eng_name:
                    continue
                if pre > 0:
                    e.wait_ge(sem, pre)
                inst = fn(e)
                if inc:
                    inst.then_inc(sem, inc)
            if eng_name == "sync":
                e.wait_ge(sem, total)

        @block.sync
        def _(e):
            emit_for("sync", e)

        @block.vector
        def _(e):
            emit_for("vector", e)

        @block.scalar
        def _(e):
            emit_for("scalar", e)

        @block.tensor
        def _(e):
            emit_for("tensor", e)

        @block.gpsimd
        def _(e):
            emit_for("gpsimd", e)


def _run(nc, in_maps):
    res = run_bass_kernel_spmd(nc, in_maps, core_ids=list(range(NC)))
    return res.results


def _bf(x):
    return np.ascontiguousarray(x, dtype=BF)


def _f32(x):
    return np.ascontiguousarray(x, dtype=np.float32)


IDENT_BF = np.eye(128, dtype=BF)
IDENT_F32 = np.eye(128, dtype=np.float32)



def build_qkv_launch(projs, bitnet, rope):
    nc = bass.Bass("TRN2", target_bir_lowering=False)
    x_in = nc.dram_tensor("x", [TC, D], F32, kind="ExternalInput").ap()
    wts = {}
    for (pname, odim) in projs:
        wts[pname] = {
            "WT": nc.dram_tensor(f"WT_{pname}", [D, odim], F32, kind="ExternalInput").ap(),
        }
        if bitnet:
            wts[pname]["Wraw"] = nc.dram_tensor(f"Wraw_{pname}", [odim, D], F32, kind="ExternalInput").ap()
            wts[pname]["scale"] = nc.dram_tensor(f"scale_{pname}", [128, odim // 128], F32, kind="ExternalInput").ap()
        wts[pname]["out"] = nc.dram_tensor(f"out_{pname}", [odim, TC], BF16, kind="ExternalOutput").ap()
    if rope:
        cos_in = nc.dram_tensor("cos_t", [128, TC], F32, kind="ExternalInput").ap()
        sin_in = nc.dram_tensor("sin_t", [128, TC], F32, kind="ExternalInput").ap()
    ident_in = nc.dram_tensor("identbf", [128, 128], BF16, kind="ExternalInput").ap()

    TB = 512
    NTB = TC // TB
    ch = Chain()
    with ExitStack() as ctx:
        sb = lambda n_, s, d: ctx.enter_context(nc.sbuf_tensor(n_, s, d))
        xt = sb("xt", [128, D], F32)
        xbf = sb("xbf", [128, D], BF16)
        xnT = sb("xnT", [128, 32 * TB], BF16)
        mu = sb("mu", [128, 1], F32)
        sq = sb("sq", [128, 1], F32)
        var = sb("var", [128, 1], F32)
        rstd = sb("rstd", [128, 1], F32)
        scr = sb("scr", [128, D], BF16)
        ident = sb("ident", [128, 128], BF16)
        wtile = sb("wtile", [128, 128 * 32], BF16)
        wf32 = sb("wf32", [128, 4096], F32)
        svec = sb("svec", [128, 64], F32)
        svtmp = sb("svtmp", [128, 1], F32)
        scl = sb("scl", [128, 64], F32)
        otile = sb("otile", [128, TB], F32)
        obf = sb("obf", [128, TB], BF16)
        cost = sb("cost", [128, TC], F32)
        sint = sb("sint", [128, TC], F32)
        rtmp = sb("rtmp", [128, TB], F32)
        rtmp2 = sb("rtmp2", [128, TB], F32)
        rsw = sb("rsw", [128, TB], F32)
        epsc = sb("epsc", [128, 1], F32)
        ptr = ctx.enter_context(nc.psum_tensor("ptr", [128, 128], BF16))
        pmm = ctx.enter_context(nc.psum_tensor("pmm", [128, TB], F32))
        sem = ctx.enter_context(nc.semaphore("chain"))
        block = ctx.enter_context(nc.Block())

        ch.dma(lambda e: e.dma_start(out=ident[:, :], in_=ident_in))
        ch.v(lambda e: e.memset(epsc[:, :], EPS))
        if rope:
            ch.dma(lambda e: e.dma_start(out=cost[:, :], in_=cos_in))
            ch.dma(lambda e: e.dma_start(out=sint[:, :], in_=sin_in))

        svec_cols = {}
        col = 0
        if bitnet:
            for (pname, odim) in projs:
                base = col
                svec_cols[pname] = base
                ch.dma(lambda e, w=wts[pname]["scale"], n_=odim // 128, base=base: e.dma_start(
                    out=scl[:, base:base + n_], in_=w))
                for jc in range(odim // 128):
                    cc = col
                    ch.dma(lambda e, w=wts[pname]["Wraw"], jc=jc: e.dma_start(
                        out=wf32[:, :], in_=w[jc * 128:(jc + 1) * 128, :]))
                    ch.v(lambda e, cc=cc: e.tensor_reduce(
                        out=svec[:, cc:cc + 1], in_=wf32[:, :], axis=AX.X,
                        op=ALU.add, apply_absolute_value=True))
                    ch.v(lambda e, cc=cc: e.tensor_scalar(
                        out=svtmp[:, :], in0=svec[:, cc:cc + 1], scalar1=1.0 / 4096.0,
                        scalar2=1e-8, op0=ALU.mult, op1=ALU.max))
                    ch.v(lambda e, cc=cc: e.tensor_tensor(
                        out=svec[:, cc:cc + 1], in0=svtmp[:, :], in1=scl[:, cc:cc + 1],
                        op=ALU.mult))
                    col += 1
        for tb in range(NTB):
            for tt in range(TB // 128):
                trow = tb * TB + tt * 128
                ch.dma(lambda e, trow=trow: e.dma_start(
                    out=xt[:, :], in_=x_in[trow:trow + 128, :]))
                ch.v(lambda e: e.tensor_reduce(out=mu[:, :], in_=xt[:, :], axis=AX.X, op=ALU.add))
                ch.v(lambda e: e.tensor_scalar(out=mu[:, :], in0=mu[:, :], scalar1=1.0 / D,
                                               scalar2=None, op0=ALU.mult))
                ch.v(lambda e: e.scalar_tensor_tensor(out=scr[:, :], in0=xt[:, :], scalar=1.0,
                                                      in1=xt[:, :], op0=ALU.mult, op1=ALU.mult,
                                                      accum_out=sq[:, :]))
                ch.v(lambda e: e.tensor_scalar(out=sq[:, :], in0=sq[:, :], scalar1=1.0 / D,
                                               scalar2=None, op0=ALU.mult))
                ch.v(lambda e: e.scalar_tensor_tensor(out=var[:, :], in0=mu[:, :], scalar=-1.0,
                                                      in1=mu[:, :], op0=ALU.mult, op1=ALU.mult))
                ch.v(lambda e: e.tensor_add(var[:, :], var[:, :], sq[:, :]))
                ch.sc(lambda e: e.activation(rstd[:, :], var[:, :], AF.Ln, bias=epsc[:, :]))
                ch.sc(lambda e: e.activation(rstd[:, :], rstd[:, :], AF.Exp, scale=-0.5))
                ch.v(lambda e: e.tensor_scalar(out=xbf[:, :], in0=xt[:, :], scalar1=mu[:, :],
                                               scalar2=rstd[:, :], op0=ALU.subtract, op1=ALU.mult))
                for c in range(32):
                    ch.te(lambda e, c=c: e.transpose(ptr[:, :], xbf[:, c * 128:(c + 1) * 128],
                                                     ident[:, :]))
                    ch.v(lambda e, c=c, tt=tt: e.tensor_copy(
                        out=xnT[:, c * TB + tt * 128: c * TB + tt * 128 + 128], in_=ptr[:, :]))
            for (pname, odim) in projs:
                wt_ap = wts[pname]["WT"]
                for jc in range(odim // 128):
                    ch.dma(lambda e, wt_ap=wt_ap, jc=jc: e.dma_start(
                        out=wtile[:, :],
                        in_=wt_ap.rearrange("(c p) j -> p c j", p=128)[:, :, jc * 128:(jc + 1) * 128]),
                        q="gpsimd")
                    if bitnet:
                        ch.sc(lambda e: e.activation(wtile[:, :], wtile[:, :], AF.Sign))
                    for dmc in range(32):
                        ch.te(lambda e, dmc=dmc: e.matmul(
                            pmm[:, :], wtile[:, dmc * 128:(dmc + 1) * 128],
                            xnT[:, dmc * TB:(dmc + 1) * TB],
                            start=(dmc == 0), stop=(dmc == 31)), inc=(1 if dmc == 31 else 0))
                    if bitnet:
                        cc = svec_cols[pname] + jc
                        ch.v(lambda e, cc=cc: e.tensor_scalar(
                            out=otile[:, :], in0=pmm[:, :], scalar1=svec[:, cc:cc + 1],
                            scalar2=None, op0=ALU.mult))
                    else:
                        ch.v(lambda e: e.tensor_copy(out=otile[:, :], in_=pmm[:, :]))
                    if rope and pname in ("q", "k"):
                        cs = cost[:, tb * TB:(tb + 1) * TB]
                        sn = sint[:, tb * TB:(tb + 1) * TB]
                        ch.v(lambda e: e.tensor_scalar(out=rsw[0:64, :], in0=otile[64:128, :],
                                                       scalar1=-1.0, scalar2=None, op0=ALU.mult))
                        ch.v(lambda e: e.tensor_copy(out=rsw[64:128, :], in_=otile[0:64, :]))
                        ch.v(lambda e, cs=cs: e.tensor_tensor(out=rtmp[:, :], in0=otile[:, :],
                                                              in1=cs, op=ALU.mult))
                        ch.v(lambda e, sn=sn: e.tensor_tensor(out=rtmp2[:, :], in0=rsw[:, :],
                                                              in1=sn, op=ALU.mult))
                        ch.v(lambda e: e.tensor_add(rtmp[:, :], rtmp[:, :], rtmp2[:, :]))
                        ch.v(lambda e: e.tensor_copy(out=obf[:, :], in_=rtmp[:, :]))
                    else:
                        ch.v(lambda e: e.tensor_copy(out=obf[:, :], in_=otile[:, :]))
                    ch.dma(lambda e, pname=pname, jc=jc, tb=tb: e.dma_start(
                        out=wts[pname]["out"][jc * 128:(jc + 1) * 128, tb * TB:(tb + 1) * TB],
                        in_=obf[:, :]))
        ch.emit(nc, block, sem)
    return nc


_QKV_CACHE = {}


def run_qkv(x_flat, weights, bitnet, rope, cos_half=None, sin_half=None):
    projs = tuple((n_, w["W"].shape[0]) for n_, w in weights.items())
    key = (projs, bitnet, rope)
    if key not in _QKV_CACHE:
        _QKV_CACHE[key] = build_qkv_launch(list(projs), bitnet, rope)
    nc = _QKV_CACHE[key]
    in_maps = []
    for c in range(NC):
        m = {"x": _f32(x_flat[c * TC:(c + 1) * TC]), "identbf": IDENT_BF}
        for n_, w in weights.items():
            m[f"WT_{n_}"] = _f32(np.ascontiguousarray(w["W"].T))
            if bitnet:
                m[f"Wraw_{n_}"] = _f32(w["W"])
                odim = w["W"].shape[0]
                m[f"scale_{n_}"] = _f32(np.ascontiguousarray(
                    w["scale"].reshape(odim // 128, 128).T))
        if rope:
            m["cos_t"] = _f32(cos_half[:, c * TC:(c + 1) * TC])
            m["sin_t"] = _f32(sin_half[:, c * TC:(c + 1) * TC])
        in_maps.append(m)
    res = _run(nc, in_maps)
    out = {}
    for n_, w in weights.items():
        odim = w["W"].shape[0]
        full = np.empty((odim, T), dtype=BF)
        for c in range(NC):
            full[:, c * TC:(c + 1) * TC] = res[c][f"out_{n_}"]
        out[n_] = full
    return out



def build_attn_launch(causal):
    nc = bass.Bass("TRN2", target_bir_lowering=False)
    PP = PAIRS_PER_CORE
    qh = nc.dram_tensor("qh", [PP * 128, S], BF16, kind="ExternalInput").ap()
    kh = nc.dram_tensor("kh", [PP * 128, S], BF16, kind="ExternalInput").ap()
    vT = nc.dram_tensor("vT", [PP * S, 128], BF16, kind="ExternalInput").ap()
    ao = nc.dram_tensor("ao", [PP * S, 128], BF16, kind="ExternalOutput").ap()
    ident_in = nc.dram_tensor("identbf", [128, 128], BF16, kind="ExternalInput").ap()
    if causal:
        cmask_in = nc.dram_tensor("cmask", [128, 4 * 512], F32, kind="ExternalInput").ap()

    NKB = S // 512
    ch = Chain()
    with ExitStack() as ctx:
        sb = lambda n_, s, d: ctx.enter_context(nc.sbuf_tensor(n_, s, d))
        qt = sb("qt", [128, S], BF16)
        kt = sb("kt", [128, S], BF16)
        vtt = sb("vtt", [128, 32 * 128], BF16)
        d_s = sb("d_s", [128, S], F32)
        p_s = sb("p_s", [128, S], BF16)
        ind = sb("ind", [128, S], BF16)
        pT = sb("pT", [128, 128], BF16)
        m_ = sb("m_", [128, 1], F32)
        mb = sb("mb", [128, 1], F32)
        lo = sb("lo", [128, 1], F32)
        hi = sb("hi", [128, 1], F32)
        mid = sb("mid", [128, 1], F32)
        hw = sb("hw", [128, 1], F32)
        cnt = sb("cnt", [128, 1], F32)
        seltk = ctx.enter_context(nc.sbuf_tensor("seltk", [128, 1], U8))
        ssum = sb("ssum", [128, 1], F32)
        rsum = sb("rsum", [128, 1], F32)
        otile = sb("otile", [128, 128], F32)
        obf = sb("obf", [128, 128], BF16)
        ident = sb("ident", [128, 128], BF16)
        cmask = sb("cmask_s", [128, 4 * 512], F32) if causal else None
        ptr = ctx.enter_context(nc.psum_tensor("ptr", [128, 128], BF16))
        pqk = ctx.enter_context(nc.psum_tensor("pqk", [128, 512], F32))
        pov = ctx.enter_context(nc.psum_tensor("pov", [128, 128], F32))
        sem = ctx.enter_context(nc.semaphore("chain"))
        block = ctx.enter_context(nc.Block())

        ch.dma(lambda e: e.dma_start(out=ident[:, :], in_=ident_in))
        if causal:
            ch.dma(lambda e: e.dma_start(out=cmask[:, :], in_=cmask_in))

        for pr in range(PP):
            ch.dma(lambda e, pr=pr: e.dma_start(out=qt[:, :], in_=qh[pr * 128:(pr + 1) * 128, :]))
            ch.dma(lambda e, pr=pr: e.dma_start(out=kt[:, :], in_=kh[pr * 128:(pr + 1) * 128, :]))
            ch.dma(lambda e, pr=pr: e.dma_start(
                out=vtt[:, :],
                in_=vT.rearrange("(pp c p) d -> p (pp c) d", p=128, pp=PP)[:, pr * 32:(pr + 1) * 32, :]),
                q="gpsimd")
            for qc in range(NQC):
                nkb = (qc * 128 // 512 + 1) if causal else NKB
                kwid = nkb * 512
                for kb in range(nkb):
                    ch.te(lambda e, qc=qc, kb=kb: e.matmul(
                        pqk[:, :], qt[:, qc * 128:(qc + 1) * 128],
                        kt[:, kb * 512:(kb + 1) * 512], start=True, stop=True))
                    ch.v(lambda e, kb=kb: e.tensor_copy(out=d_s[:, kb * 512:(kb + 1) * 512],
                                                        in_=pqk[:, :]))
                if causal:
                    ch.v(lambda e, kwid=kwid: e.tensor_reduce(
                        out=lo[:, :], in_=d_s[:, 0:kwid], axis=AX.X, op=ALU.min))
                    off = (qc % 4) * 512
                    dg = (nkb - 1) * 512
                    ch.v(lambda e, off=off, dg=dg: e.tensor_tensor(
                        out=d_s[:, dg:dg + 512], in0=d_s[:, dg:dg + 512],
                        in1=cmask[:, off:off + 512], op=ALU.add))
                ch.v(lambda e, kwid=kwid: e.tensor_reduce(
                    out=m_[:, :], in_=d_s[:, 0:kwid], axis=AX.X, op=ALU.max))
                if causal:
                    ch.v(lambda e: e.scalar_tensor_tensor(
                        out=mid[:, :], in0=lo[:, :], scalar=1.0, in1=m_[:, :],
                        op0=ALU.mult, op1=ALU.add))
                    ch.v(lambda e: e.tensor_scalar(
                        out=mid[:, :], in0=mid[:, :], scalar1=0.5, scalar2=None, op0=ALU.mult))
                    ch.v(lambda e: e.scalar_tensor_tensor(
                        out=hw[:, :], in0=lo[:, :], scalar=-1.0, in1=m_[:, :],
                        op0=ALU.mult, op1=ALU.add))
                    ch.v(lambda e: e.tensor_scalar(
                        out=hw[:, :], in0=hw[:, :], scalar1=0.5, scalar2=None, op0=ALU.mult))
                    for it in range(20):
                        ch.v(lambda e, kwid=kwid: e.tensor_scalar(
                            out=ind[:, 0:kwid], in0=d_s[:, 0:kwid], scalar1=mid[:, :],
                            scalar2=None, op0=ALU.is_ge, op1=ALU.add, accum_out=cnt[:, :]))
                        ch.v(lambda e: e.tensor_scalar(
                            out=hw[:, :], in0=hw[:, :], scalar1=0.5, scalar2=None, op0=ALU.mult))
                        ch.v(lambda e: e.tensor_scalar(
                            out=cnt[:, :], in0=cnt[:, :], scalar1=float(NUM_REMAIN),
                            scalar2=2.0, op0=ALU.is_ge, op1=ALU.mult))
                        ch.v(lambda e: e.scalar_tensor_tensor(
                            out=cnt[:, :], in0=cnt[:, :], scalar=-1.0, in1=hw[:, :],
                            op0=ALU.add, op1=ALU.mult))
                        ch.v(lambda e: e.tensor_add(mid[:, :], mid[:, :], cnt[:, :]))
                    ch.v(lambda e: e.scalar_tensor_tensor(
                        out=mid[:, :], in0=hw[:, :], scalar=-4.0, in1=mid[:, :],
                        op0=ALU.mult, op1=ALU.add))
                    ch.v(lambda e, kwid=kwid: e.tensor_scalar(
                        out=ind[:, 0:kwid], in0=d_s[:, 0:kwid], scalar1=mid[:, :],
                        scalar2=None, op0=ALU.is_ge))
                ch.v(lambda e: e.tensor_scalar(
                    out=mb[:, :], in0=m_[:, :], scalar1=-ISR, scalar2=None, op0=ALU.mult))
                if causal:
                    ch.sc(lambda e, kwid=kwid: e.activation(
                        p_s[:, 0:kwid], d_s[:, 0:kwid], AF.Exp, bias=mb[:, :], scale=ISR))
                    ch.v(lambda e, kwid=kwid: e.scalar_tensor_tensor(
                        out=p_s[:, 0:kwid], in0=p_s[:, 0:kwid], scalar=1.0,
                        in1=ind[:, 0:kwid], op0=ALU.mult, op1=ALU.mult,
                        accum_out=ssum[:, :]))
                else:
                    ch.sc(lambda e, kwid=kwid: e.activation(
                        p_s[:, 0:kwid], d_s[:, 0:kwid], AF.Exp, bias=mb[:, :], scale=ISR,
                        accum_out=ssum[:, :]))
                ch.v(lambda e: e.reciprocal(rsum[:, :], ssum[:, :]))
                nkc = kwid // 128
                for kc in range(nkc):
                    ch.te(lambda e, kc=kc: e.transpose(
                        ptr[:, :], p_s[:, kc * 128:(kc + 1) * 128], ident[:, :]))
                    ch.v(lambda e: e.tensor_copy(out=pT[:, :], in_=ptr[:, :]))
                    ch.te(lambda e, kc=kc, first=(kc == 0), last=(kc == nkc - 1): e.matmul(
                        pov[:, :], pT[:, :], vtt[:, kc * 128:(kc + 1) * 128],
                        start=first, stop=last), inc=1)
                ch.v(lambda e: e.tensor_scalar(
                    out=obf[:, :], in0=pov[:, :], scalar1=rsum[:, :], scalar2=None,
                    op0=ALU.mult))
                ch.dma(lambda e, pr=pr, qc=qc: e.dma_start(
                    out=ao[pr * S + qc * 128: pr * S + (qc + 1) * 128, :], in_=obf[:, :]))
        ch.emit(nc, block, sem)
    return nc


_ATTN_CACHE = {}


CMASK = None

def _cmask():
    global CMASK
    if CMASK is None:
        m = np.zeros((128, 4, 512), np.float32)
        for off in range(4):
            p = np.arange(128)[:, None]
            j = np.arange(512)[None, :]
            m[:, off, :] = np.where(p + off * 128 - j >= 0, 0.0, NEG)
        CMASK = np.ascontiguousarray(m.reshape(128, 2048))
    return CMASK


def run_attn(qh, kh, vth, causal):
    if causal not in _ATTN_CACHE:
        _ATTN_CACHE[causal] = build_attn_launch(causal)
    nc = _ATTN_CACHE[causal]
    PP = PAIRS_PER_CORE
    in_maps = []
    for c in range(NC):
        in_maps.append({
            "qh": qh[c * PP * 128:(c + 1) * PP * 128],
            "kh": kh[c * PP * 128:(c + 1) * PP * 128],
            "vT": vth[c * PP * S:(c + 1) * PP * S],
            "identbf": IDENT_BF,
        })
        if causal:
            in_maps[-1]["cmask"] = _cmask()
    res = _run(nc, in_maps)
    ao = np.empty((B * HEADS * S, 128), dtype=BF)
    for c in range(NC):
        ao[c * PP * S:(c + 1) * PP * S] = res[c]["ao"]
    return ao



def emit_ln_stats(ch, src, mu, sq, var, rstd, scr, epsc, width):
    ch.v(lambda e: e.tensor_reduce(out=mu[:, :], in_=src, axis=AX.X, op=ALU.add))
    ch.v(lambda e: e.tensor_scalar(out=mu[:, :], in0=mu[:, :], scalar1=1.0 / width,
                                   scalar2=None, op0=ALU.mult))
    ch.v(lambda e: e.scalar_tensor_tensor(out=scr, in0=src, scalar=1.0,
                                          in1=src, op0=ALU.mult, op1=ALU.mult,
                                          accum_out=sq[:, :]))
    ch.v(lambda e: e.tensor_scalar(out=sq[:, :], in0=sq[:, :], scalar1=1.0 / width,
                                   scalar2=None, op0=ALU.mult))
    ch.v(lambda e: e.scalar_tensor_tensor(out=var[:, :], in0=mu[:, :], scalar=-1.0,
                                          in1=mu[:, :], op0=ALU.mult, op1=ALU.mult))
    ch.v(lambda e: e.tensor_add(var[:, :], var[:, :], sq[:, :]))
    ch.sc(lambda e: e.activation(rstd[:, :], var[:, :], AF.Ln, bias=epsc[:, :]))
    ch.sc(lambda e: e.activation(rstd[:, :], rstd[:, :], AF.Exp, scale=-0.5))



def build_out_launch(bitnet):
    nc = bass.Bass("TRN2", target_bir_lowering=False)
    aT_in = nc.dram_tensor("aT", [D, TC], BF16, kind="ExternalInput").ap()
    wt_in = nc.dram_tensor("WT", [D, D], F32, kind="ExternalInput").ap()
    if bitnet:
        wraw_in = nc.dram_tensor("Wraw", [D, D], F32, kind="ExternalInput").ap()
        scale_in = nc.dram_tensor("scale", [128, 32], F32, kind="ExternalInput").ap()
    x_in = nc.dram_tensor("x", [TC, D], F32, kind="ExternalInput").ap()
    xo_out = nc.dram_tensor("xo", [TC, D], F32, kind="ExternalOutput").ap()
    identf_in = nc.dram_tensor("identf", [128, 128], F32, kind="ExternalInput").ap()

    HALF = 512
    ch = Chain()
    with ExitStack() as ctx:
        sb = lambda n_, s, d: ctx.enter_context(nc.sbuf_tensor(n_, s, d))
        aTs = sb("aTs", [128, 32 * HALF], BF16)
        wtile = sb("wtile", [128, 128 * 32], BF16)
        wf32 = sb("wf32", [128, 4096], F32)
        svec = sb("svec", [128, 32], F32)
        svtmp = sb("svtmp", [128, 1], F32)
        scl = sb("scl", [128, 32], F32)
        ftile = sb("ftile", [128, 128], F32)
        ftok = sb("ftok", [128, 4 * D], F32)
        xt = sb("xt", [128, D], F32)
        scr = sb("scr", [128, D], BF16)
        mu = sb("mu", [128, 1], F32)
        sq = sb("sq", [128, 1], F32)
        var = sb("var", [128, 1], F32)
        rstd = sb("rstd", [128, 1], F32)
        epsc = sb("epsc", [128, 1], F32)
        identf = sb("identf_s", [128, 128], F32)
        ptrf = ctx.enter_context(nc.psum_tensor("ptrf", [128, 128], F32))
        pmm = ctx.enter_context(nc.psum_tensor("pmm", [128, 128], F32))
        sem = ctx.enter_context(nc.semaphore("chain"))
        block = ctx.enter_context(nc.Block())

        ch.dma(lambda e: e.dma_start(out=identf[:, :], in_=identf_in))
        ch.v(lambda e: e.memset(epsc[:, :], EPS))
        if bitnet:
            ch.dma(lambda e: e.dma_start(out=scl[:, :], in_=scale_in))
            for jc in range(32):
                ch.dma(lambda e, jc=jc: e.dma_start(
                    out=wf32[:, :], in_=wraw_in[jc * 128:(jc + 1) * 128, :]))
                ch.v(lambda e, jc=jc: e.tensor_reduce(
                    out=svec[:, jc:jc + 1], in_=wf32[:, :], axis=AX.X,
                    op=ALU.add, apply_absolute_value=True))
                ch.v(lambda e, jc=jc: e.tensor_scalar(
                    out=svtmp[:, :], in0=svec[:, jc:jc + 1], scalar1=1.0 / 4096.0,
                    scalar2=1e-8, op0=ALU.mult, op1=ALU.max))
                ch.v(lambda e, jc=jc: e.tensor_tensor(
                    out=svec[:, jc:jc + 1], in0=svtmp[:, :], in1=scl[:, jc:jc + 1],
                    op=ALU.mult))
        for half in range(TC // HALF):
            ch.dma(lambda e, half=half: e.dma_start(
                out=aTs[:, :],
                in_=aT_in.rearrange("(c p) t -> p c t", p=128)[:, :, half * HALF:(half + 1) * HALF]))
            for jc in range(32):
                ch.dma(lambda e, jc=jc: e.dma_start(
                    out=wtile[:, :],
                    in_=wt_in.rearrange("(c p) j -> p c j", p=128)[:, :, jc * 128:(jc + 1) * 128]),
                    q="gpsimd")
                if bitnet:
                    ch.sc(lambda e: e.activation(wtile[:, :], wtile[:, :], AF.Sign))
                for tt in range(HALF // 128):
                    for dc in range(32):
                        ch.te(lambda e, dc=dc, tt=tt: e.matmul(
                            pmm[:, :], wtile[:, dc * 128:(dc + 1) * 128],
                            aTs[:, dc * HALF + tt * 128: dc * HALF + tt * 128 + 128],
                            start=(dc == 0), stop=(dc == 31)), inc=(1 if dc == 31 else 0))
                    if bitnet:
                        ch.v(lambda e, jc=jc: e.tensor_scalar(
                            out=ftile[:, :], in0=pmm[:, :], scalar1=svec[:, jc:jc + 1],
                            scalar2=None, op0=ALU.mult))
                    else:
                        ch.v(lambda e: e.tensor_copy(out=ftile[:, :], in_=pmm[:, :]))
                    ch.te(lambda e: e.transpose(ptrf[:, :], ftile[:, :], identf[:, :]))
                    ch.v(lambda e, tt=tt, jc=jc: e.tensor_copy(
                        out=ftok[:, tt * D + jc * 128: tt * D + jc * 128 + 128],
                        in_=ptrf[:, :]))
            for tt in range(HALF // 128):
                trow = half * HALF + tt * 128
                fv = ftok[:, tt * D:(tt + 1) * D]
                emit_ln_stats(ch, fv, mu, sq, var, rstd, scr[:, :], epsc, D)
                ch.dma(lambda e, trow=trow: e.dma_start(
                    out=xt[:, :], in_=x_in[trow:trow + 128, :]))
                ch.v(lambda e, fv=fv: e.tensor_scalar(
                    out=fv, in0=fv, scalar1=mu[:, :], scalar2=rstd[:, :],
                    op0=ALU.subtract, op1=ALU.mult))
                ch.v(lambda e, fv=fv: e.tensor_add(xt[:, :], xt[:, :], fv))
                ch.dma(lambda e, trow=trow: e.dma_start(
                    out=xo_out[trow:trow + 128, :], in_=xt[:, :]))
        ch.emit(nc, block, sem)
    return nc


_OUT_CACHE = {}


def run_out(aT, W, scale, x_flat, bitnet):
    if bitnet not in _OUT_CACHE:
        _OUT_CACHE[bitnet] = build_out_launch(bitnet)
    nc = _OUT_CACHE[bitnet]
    WT = _f32(np.ascontiguousarray(W.T))
    if bitnet:
        scl = _f32(np.ascontiguousarray(scale.reshape(32, 128).T))
    in_maps = []
    for c in range(NC):
        m = {
            "aT": np.ascontiguousarray(aT[:, c * TC:(c + 1) * TC]),
            "WT": WT,
            "x": _f32(x_flat[c * TC:(c + 1) * TC]),
            "identf": IDENT_F32,
        }
        if bitnet:
            m["Wraw"] = _f32(W)
            m["scale"] = scl
        in_maps.append(m)
    res = _run(nc, in_maps)
    xo = np.empty((T, D), dtype=np.float32)
    for c in range(NC):
        xo[c * TC:(c + 1) * TC] = res[c]["xo"]
    return xo



def build_ffn_launch(variant):
    nc = bass.Bass("TRN2", target_bir_lowering=False)
    x_in = nc.dram_tensor("x", [TC, D], F32, kind="ExternalInput").ap()
    wt_in = nc.dram_tensor("WT", [D, D], F32, kind="ExternalInput").ap()
    bias_in = nc.dram_tensor("bias", [128, 32], F32, kind="ExternalInput").ap()
    NER = 256 if variant == "l0" else 128
    at_in = nc.dram_tensor("AT", [D, NER], F32, kind="ExternalInput").ap()
    b2_in = nc.dram_tensor("B2", [NER, D], F32, kind="ExternalInput").ap()
    if variant == "l0":
        coeffs_in = nc.dram_tensor("coeffs", [16, 1], F32, kind="ExternalInput").ap()
        scales_in = nc.dram_tensor("scales", [16, 1], F32, kind="ExternalInput").ap()
        m1_in = nc.dram_tensor("M1", [16, 256], F32, kind="ExternalInput").ap()
    else:
        gwt_in = nc.dram_tensor("gwT", [D, 8], F32, kind="ExternalInput").ap()
        m2_in = nc.dram_tensor("M2", [8, 128], F32, kind="ExternalInput").ap()
    identb_in = nc.dram_tensor("identbf", [128, 128], BF16, kind="ExternalInput").ap()
    identf_in = nc.dram_tensor("identf", [128, 128], F32, kind="ExternalInput").ap()
    xo_out = nc.dram_tensor("xo", [TC, D], F32, kind="ExternalOutput").ap()

    HALF = 512
    NERC = NER // 128
    ch = Chain()
    with ExitStack() as ctx:
        sb = lambda n_, s, d: ctx.enter_context(nc.sbuf_tensor(n_, s, d))
        xt = sb("xt", [128, D], F32)
        xbf = sb("xbf", [128, D], BF16)
        xnT = sb("xnT", [128, 32 * HALF], BF16)
        mu = sb("mu", [128, 1], F32)
        sq = sb("sq", [128, 1], F32)
        var = sb("var", [128, 1], F32)
        rstd = sb("rstd", [128, 1], F32)
        scr = sb("scr", [128, D], BF16)
        identb = sb("identb_s", [128, 128], BF16)
        identf = sb("identf_s", [128, 128], F32)
        epsc = sb("epsc", [128, 1], F32)
        wtile = sb("wtile", [128, 128 * 32], BF16)
        biasv = sb("biasv", [128, 32], F32)
        at_s = [sb(f"at{i}", [128, 32 * 128], BF16) for i in range(NERC)]
        b2_s = [sb(f"b2{i}", [128, D], BF16) for i in range(NERC)]
        xa_s = sb("xa_s", [128, NERC * HALF], BF16)
        ftile = sb("ftile", [128, HALF], F32)
        ftok = sb("ftok", [128, 4 * D], F32)
        if variant == "l0":
            c16 = sb("c16", [16, 1], F32)
            s16 = sb("s16", [16, 1], F32)
            m1_s = sb("m1_s", [16, 256], F32)
            wex = sb("wex", [128, NERC], F32)
        else:
            gw_s = sb("gw_s", [128, 32 * 8], BF16)
            m2_s = sb("m2_s", [8, 128], F32)
            gl = sb("gl", [128, 8], F32)
            top8 = sb("top8", [128, 8], F32)
            ed = sb("ed", [128, 1], F32)
            ee = sb("ee", [128, 1], F32)
            w1 = sb("w1", [128, 1], F32)
            w2 = sb("w2", [128, 1], F32)
            wfull = sb("wfull", [128, 8], F32)
            wf2 = sb("wf2", [128, 8], F32)
            wT_s = sb("wT_s", [8, 128], F32)
            we_s = sb("we_s", [128, 4 * 128], F32)
        ptrb = ctx.enter_context(nc.psum_tensor("ptrb", [128, 128], BF16))
        ptrf = ctx.enter_context(nc.psum_tensor("ptrf", [128, 128], F32))
        pmm = ctx.enter_context(nc.psum_tensor("pmm", [128, HALF], F32))
        pxa = ctx.enter_context(nc.psum_tensor("pxa", [128, HALF], F32))
        paux = ctx.enter_context(nc.psum_tensor("paux", [128, 128], F32))
        sem = ctx.enter_context(nc.semaphore("chain"))
        block = ctx.enter_context(nc.Block())

        ch.dma(lambda e: e.dma_start(out=identb[:, :], in_=identb_in))
        ch.dma(lambda e: e.dma_start(out=identf[:, :], in_=identf_in))
        ch.dma(lambda e: e.dma_start(out=biasv[:, :], in_=bias_in))
        ch.v(lambda e: e.memset(epsc[:, :], EPS))
        for erc in range(NERC):
            ch.dma(lambda e, erc=erc: e.dma_start(
                out=at_s[erc][:, :],
                in_=at_in.rearrange("(c p) j -> p c j", p=128)[:, :, erc * 128:(erc + 1) * 128]),
                q="gpsimd")
            ch.dma(lambda e, erc=erc: e.dma_start(
                out=b2_s[erc][:, :], in_=b2_in[erc * 128:(erc + 1) * 128, :]), q="gpsimd")
        if variant == "l0":
            ch.dma(lambda e: e.dma_start(out=c16[:, :], in_=coeffs_in))
            ch.dma(lambda e: e.dma_start(out=s16[:, :], in_=scales_in))
            ch.dma(lambda e: e.dma_start(out=m1_s[:, :], in_=m1_in))
            ch.v(lambda e: e.tensor_tensor(out=c16[:, :], in0=c16[:, :], in1=s16[:, :],
                                           op=ALU.mult))
            for erc in range(NERC):
                ch.te(lambda e, erc=erc: e.matmul(
                    paux[:, 0:1], m1_s[:, erc * 128:(erc + 1) * 128], c16[:, :],
                    start=True, stop=True))
                ch.v(lambda e, erc=erc: e.tensor_copy(out=wex[:, erc:erc + 1],
                                                      in_=paux[:, 0:1]))
        else:
            ch.dma(lambda e: e.dma_start(
                out=gw_s[:, :], in_=gwt_in.rearrange("(c p) j -> p c j", p=128)),
                q="gpsimd")
            ch.dma(lambda e: e.dma_start(out=m2_s[:, :], in_=m2_in))

        for half in range(TC // HALF):
            for tt in range(HALF // 128):
                trow = half * HALF + tt * 128
                ch.dma(lambda e, trow=trow: e.dma_start(
                    out=xt[:, :], in_=x_in[trow:trow + 128, :]))
                emit_ln_stats(ch, xt[:, :], mu, sq, var, rstd, scr[:, :], epsc, D)
                ch.v(lambda e: e.tensor_scalar(out=xbf[:, :], in0=xt[:, :], scalar1=mu[:, :],
                                               scalar2=rstd[:, :], op0=ALU.subtract, op1=ALU.mult))
                for c in range(32):
                    ch.te(lambda e, c=c: e.transpose(ptrb[:, :], xbf[:, c * 128:(c + 1) * 128],
                                                     identb[:, :]))
                    ch.v(lambda e, c=c, tt=tt: e.tensor_copy(
                        out=xnT[:, c * HALF + tt * 128: c * HALF + tt * 128 + 128],
                        in_=ptrb[:, :]))
            if variant == "l1":
                for tt in range(HALF // 128):
                    for dc in range(32):
                        ch.te(lambda e, dc=dc, tt=tt: e.matmul(
                            paux[0:128, 0:8], xnT[:, dc * HALF + tt * 128: dc * HALF + tt * 128 + 128],
                            gw_s[:, dc * 8:(dc + 1) * 8],
                            start=(dc == 0), stop=(dc == 31)), inc=(1 if dc == 31 else 0))
                    ch.v(lambda e: e.tensor_copy(out=gl[:, :], in_=paux[0:128, 0:8]))
                    ch.v(lambda e: e.max(out=top8[:, :], in_=gl[:, :]))
                    ch.v(lambda e: e.tensor_sub(ed[:, :], top8[:, 1:2], top8[:, 0:1]))
                    ch.sc(lambda e: e.activation(ee[:, :], ed[:, :], AF.Exp))
                    ch.v(lambda e: e.tensor_scalar(out=w1[:, :], in0=ee[:, :], scalar1=1.0,
                                                   scalar2=None, op0=ALU.add))
                    ch.v(lambda e: e.reciprocal(w1[:, :], w1[:, :]))
                    ch.v(lambda e: e.tensor_mul(w2[:, :], ee[:, :], w1[:, :]))
                    ch.v(lambda e: e.tensor_scalar(out=wfull[:, :], in0=gl[:, :],
                                                   scalar1=top8[:, 0:1], scalar2=w1[:, :],
                                                   op0=ALU.is_equal, op1=ALU.mult))
                    ch.v(lambda e: e.tensor_scalar(out=wf2[:, :], in0=gl[:, :],
                                                   scalar1=top8[:, 1:2], scalar2=w2[:, :],
                                                   op0=ALU.is_equal, op1=ALU.mult))
                    ch.v(lambda e: e.tensor_add(wfull[:, :], wfull[:, :], wf2[:, :]))
                    ch.te(lambda e: e.transpose(paux[0:8, 0:128], wfull[:, :], identf[:, :]))
                    ch.v(lambda e: e.tensor_copy(out=wT_s[:, :], in_=paux[0:8, 0:128]))
                    ch.te(lambda e: e.matmul(paux[0:128, 0:128], m2_s[:, :], wT_s[:, :],
                                             start=True, stop=True))
                    ch.v(lambda e, tt=tt: e.tensor_copy(
                        out=we_s[:, tt * 128:(tt + 1) * 128], in_=paux[0:128, 0:128]))
            for erc in range(NERC):
                for dc in range(32):
                    ch.te(lambda e, erc=erc, dc=dc: e.matmul(
                        pxa[:, :], at_s[erc][:, dc * 128:(dc + 1) * 128],
                        xnT[:, dc * HALF:(dc + 1) * HALF],
                        start=(dc == 0), stop=(dc == 31)), inc=(1 if dc == 31 else 0))
                if variant == "l0":
                    ch.v(lambda e, erc=erc: e.tensor_scalar(
                        out=xa_s[:, erc * HALF:(erc + 1) * HALF], in0=pxa[:, :],
                        scalar1=wex[:, erc:erc + 1], scalar2=None, op0=ALU.mult))
                else:
                    for tt in range(HALF // 128):
                        ch.v(lambda e, erc=erc, tt=tt: e.tensor_tensor(
                            out=xa_s[:, erc * HALF + tt * 128: erc * HALF + tt * 128 + 128],
                            in0=pxa[:, tt * 128:(tt + 1) * 128],
                            in1=we_s[:, tt * 128:(tt + 1) * 128], op=ALU.mult))
            for jc in range(32):
                ch.dma(lambda e, jc=jc: e.dma_start(
                    out=wtile[:, :],
                    in_=wt_in.rearrange("(c p) j -> p c j", p=128)[:, :, jc * 128:(jc + 1) * 128]),
                    q="gpsimd")
                for dc in range(32):
                    ch.te(lambda e, dc=dc: e.matmul(
                        pmm[:, :], wtile[:, dc * 128:(dc + 1) * 128],
                        xnT[:, dc * HALF:(dc + 1) * HALF],
                        start=(dc == 0), stop=False), inc=0)
                for erc in range(NERC):
                    ch.te(lambda e, erc=erc, jc=jc: e.matmul(
                        pmm[:, :], b2_s[erc][:, jc * 128:(jc + 1) * 128],
                        xa_s[:, erc * HALF:(erc + 1) * HALF],
                        start=False, stop=(erc == NERC - 1)), inc=(1 if erc == NERC - 1 else 0))
                ch.v(lambda e, jc=jc: e.tensor_scalar(
                    out=ftile[:, :], in0=pmm[:, :], scalar1=biasv[:, jc:jc + 1],
                    scalar2=None, op0=ALU.add))
                for tt in range(HALF // 128):
                    ch.te(lambda e, tt=tt: e.transpose(
                        ptrf[:, :], ftile[:, tt * 128:(tt + 1) * 128], identf[:, :]))
                    ch.v(lambda e, tt=tt, jc=jc: e.tensor_copy(
                        out=ftok[:, tt * D + jc * 128: tt * D + jc * 128 + 128],
                        in_=ptrf[:, :]))
            for tt in range(HALF // 128):
                trow = half * HALF + tt * 128
                fv = ftok[:, tt * D:(tt + 1) * D]
                emit_ln_stats(ch, fv, mu, sq, var, rstd, scr[:, :], epsc, D)
                ch.dma(lambda e, trow=trow: e.dma_start(
                    out=xt[:, :], in_=x_in[trow:trow + 128, :]))
                ch.v(lambda e, fv=fv: e.tensor_scalar(
                    out=fv, in0=fv, scalar1=mu[:, :], scalar2=rstd[:, :],
                    op0=ALU.subtract, op1=ALU.mult))
                ch.v(lambda e, fv=fv: e.tensor_add(xt[:, :], xt[:, :], fv))
                ch.dma(lambda e, trow=trow: e.dma_start(
                    out=xo_out[trow:trow + 128, :], in_=xt[:, :]))
        ch.emit(nc, block, sem)
    return nc


_FFN_CACHE = {}


def run_ffn_l0(x_flat, W, bias, A, Bf, coeffs, scales):
    if "l0" not in _FFN_CACHE:
        _FFN_CACHE["l0"] = build_ffn_launch("l0")
    nc = _FFN_CACHE["l0"]
    WT = _f32(np.ascontiguousarray(W.T))
    AT = _f32(np.ascontiguousarray(A.reshape(256, D).T))
    B2 = _f32(np.ascontiguousarray(Bf.transpose(0, 2, 1).reshape(256, D)))
    biasm = _f32(np.ascontiguousarray(bias.reshape(32, 128).T))
    M1 = np.zeros((16, 256), np.float32)
    for e_ in range(16):
        M1[e_, e_ * 16:(e_ + 1) * 16] = 1.0
    in_maps = []
    for c in range(NC):
        in_maps.append({
            "x": _f32(x_flat[c * TC:(c + 1) * TC]), "WT": WT, "bias": biasm,
            "AT": AT, "B2": B2, "coeffs": _f32(coeffs.reshape(16, 1)),
            "scales": _f32(scales.reshape(16, 1)), "M1": M1,
            "identbf": IDENT_BF, "identf": IDENT_F32,
        })
    res = _run(nc, in_maps)
    xo = np.empty((T, D), dtype=np.float32)
    for c in range(NC):
        xo[c * TC:(c + 1) * TC] = res[c]["xo"]
    return xo


def run_ffn_l1(x_flat, W, bias, lora_A, lora_B, gate_w):
    if "l1" not in _FFN_CACHE:
        _FFN_CACHE["l1"] = build_ffn_launch("l1")
    nc = _FFN_CACHE["l1"]
    WT = _f32(np.ascontiguousarray(W.T))
    AT = _f32(np.ascontiguousarray(lora_A.reshape(128, D).T))
    B2 = _f32(np.ascontiguousarray(lora_B.transpose(0, 2, 1).reshape(128, D)))
    biasm = _f32(np.ascontiguousarray(bias.reshape(32, 128).T))
    gwT = _f32(np.ascontiguousarray(gate_w.T))
    M2 = np.zeros((8, 128), np.float32)
    for e_ in range(8):
        M2[e_, e_ * 16:(e_ + 1) * 16] = 2.0
    in_maps = []
    for c in range(NC):
        in_maps.append({
            "x": _f32(x_flat[c * TC:(c + 1) * TC]), "WT": WT, "bias": biasm,
            "AT": AT, "B2": B2, "gwT": gwT, "M2": M2,
            "identbf": IDENT_BF, "identf": IDENT_F32,
        })
    res = _run(nc, in_maps)
    xo = np.empty((T, D), dtype=np.float32)
    for c in range(NC):
        xo[c * TC:(c + 1) * TC] = res[c]["xo"]
    return xo



def _heads_q(q0):
    return np.ascontiguousarray(
        q0.reshape(HEADS, 128, B, S).transpose(2, 0, 1, 3).reshape(B * HEADS * 128, S))


def _heads_kv(k0):
    r = k0.reshape(KVH, 128, B, S).transpose(2, 0, 1, 3)
    r = np.repeat(r, NREP, axis=1)
    return np.ascontiguousarray(r.reshape(B * HEADS * 128, S))


def _heads_vT(v0):
    r = v0.reshape(KVH, 128, B, S).transpose(2, 0, 3, 1)
    r = np.repeat(r, NREP, axis=1)
    return np.ascontiguousarray(r.reshape(B * HEADS * S, 128))


def _gather_heads(ao):
    a = ao.reshape(B, HEADS, S, 128).transpose(1, 3, 0, 2)
    return np.ascontiguousarray(a.reshape(D, T))


_ROPE = None


def _rope_tables():
    global _ROPE
    if _ROPE is None:
        i = np.arange(0, 128, 2, dtype=np.float64) / 128.0
        inv32 = (1.0 / (10000.0 ** i)).astype(np.float32)
        t32 = np.arange(S, dtype=np.float32)
        freqs32 = t32[:, None] * inv32[None, :]
        cos = np.cos(freqs32.astype(np.float64)).astype(np.float32)
        sin = np.sin(freqs32.astype(np.float64)).astype(np.float32)
        cos_t = np.concatenate([cos, cos], axis=1).T
        sin_t = np.concatenate([sin, sin], axis=1).T
        cos_full = np.tile(cos_t, (1, B))
        sin_full = np.tile(sin_t, (1, B))
        _ROPE = (np.ascontiguousarray(cos_full), np.ascontiguousarray(sin_full))
    return _ROPE


def kernel(L0_attn_hidden_states, L0_attn_q_weight, L0_attn_q_scale,
           L0_attn_k_weight, L0_attn_k_scale, L0_attn_v_weight, L0_attn_v_scale,
           L0_attn_o_weight, L0_attn_o_scale, L0_ffn_W, L0_ffn_bias, L0_ffn_A,
           L0_ffn_Bf, L0_ffn_coeffs, L0_ffn_scales, L1_attn_Wq, L1_attn_Wk,
           L1_attn_Wv, L1_attn_Wo, L1_ffn_gate_w, L1_ffn_base_w, L1_ffn_base_b,
           L1_ffn_lora_A, L1_ffn_lora_B):
    x = _f32(np.asarray(L0_attn_hidden_states).reshape(T, D))

    pr = run_qkv(x, {
        "q": {"W": np.asarray(L0_attn_q_weight), "scale": np.asarray(L0_attn_q_scale)},
        "k": {"W": np.asarray(L0_attn_k_weight), "scale": np.asarray(L0_attn_k_scale)},
        "v": {"W": np.asarray(L0_attn_v_weight), "scale": np.asarray(L0_attn_v_scale)},
    }, bitnet=True, rope=False)
    ao = run_attn(_heads_q(pr["q"]), _heads_kv(pr["k"]), _heads_vT(pr["v"]), causal=False)
    x = run_out(_gather_heads(ao), np.asarray(L0_attn_o_weight),
                np.asarray(L0_attn_o_scale), x, bitnet=True)

    x = run_ffn_l0(x, np.asarray(L0_ffn_W), np.asarray(L0_ffn_bias),
                   np.asarray(L0_ffn_A), np.asarray(L0_ffn_Bf),
                   np.asarray(L0_ffn_coeffs), np.asarray(L0_ffn_scales))

    cos_full, sin_full = _rope_tables()
    pr = run_qkv(x, {
        "q": {"W": np.asarray(L1_attn_Wq)},
        "k": {"W": np.asarray(L1_attn_Wk)},
        "v": {"W": np.asarray(L1_attn_Wv)},
    }, bitnet=False, rope=True, cos_half=cos_full, sin_half=sin_full)
    ao = run_attn(_heads_q(pr["q"]), _heads_kv(pr["k"]), _heads_vT(pr["v"]), causal=True)
    x = run_out(_gather_heads(ao), np.asarray(L1_attn_Wo), None, x, bitnet=False)

    x = run_ffn_l1(x, np.asarray(L1_ffn_base_w), np.asarray(L1_ffn_base_b),
                   np.asarray(L1_ffn_lora_A), np.asarray(L1_ffn_lora_B),
                   np.asarray(L1_ffn_gate_w))

    return x.reshape(B, S, D)
